# version 02 (final)
# baseline (speedup 1.0000x reference)
import numpy as np
from contextlib import ExitStack

import concourse.bass as bass
import concourse.mybir as mybir
from concourse.bass_utils import run_bass_kernel_spmd
from concourse.alu_op_type import AluOpType as OP

F32 = mybir.dt.float32
BF16 = mybir.dt.bfloat16
I32 = mybir.dt.int32
AF = mybir.ActivationFunctionType
AX = mybir.AxisListType

NCORES = 8
CUT = "none"
T = 2048
C = 1024
NCH = C // 128
TCH = T // 128
HID0 = 3072
PAIRS = [[0, 1], [2, 3], [4, 5], [6, 7]]
ENGS = ("tensor", "vector", "scalar", "gpsimd", "sync")


class Bld:

    def __init__(self):
        self.prog = {e: [] for e in ENGS}
        self.esem = {"tensor": "Et", "vector": "Ev", "scalar": "Es",
                     "gpsimd": "Eg", "sync": "Ey"}
        self.pools = {"ld": [f"Da{i}" for i in range(6)],
                      "gp": [f"Db{i}" for i in range(3)],
                      "out": [f"Do{i}" for i in range(2)]}
        self.pool_rr = {p: 0 for p in self.pools}
        dma_sems = [s for v in self.pools.values() for s in v]
        self.sem_names = set(self.esem.values()) | set(dma_sems) | {"CC"}
        self.cnt = {s: 0 for s in self.sem_names}
        self.waited = {e: {} for e in ENGS}
        self.need = {s: set() for s in self.sem_names}
        self.kstep = {s: (16 if s[0] == "D" and s != "CC" else 1)
                      for s in self.sem_names}

    def lane(self, pool):
        sems = self.pools[pool]
        i = self.pool_rr[pool] % len(sems)
        self.pool_rr[pool] += 1
        return sems[i]

    def _waits(self, eng, deps):
        out = []
        for d in deps:
            if d is None:
                continue
            s, v = d
            if s == self.esem[eng] and eng == "tensor":
                continue
            w = self.waited[eng]
            if w.get(s, 0) >= v:
                continue
            w[s] = v
            self.need[s].add(v)
            out.append((s, v))
        return out

    def op(self, eng, fn, deps=(), sem=None, k=None):
        waits = self._waits(eng, deps)
        if sem is None:
            sem = self.esem[eng]
        self.cnt[sem] += 1
        myseq = self.cnt[sem]
        if sem[0] == "D" and sem != "CC":
            self.need[sem].add(myseq)

        def emit(e, sems, waits=tuple(waits), fn=fn, sem=sem, myseq=myseq):
            for ws, wv in waits:
                e.wait_ge(sems[ws], self.val(ws, wv))
            ins = fn(e, sems)
            if myseq in self.need[sem]:
                ins.then_inc(sems[sem], self.kstep[sem])

        self.prog[eng].append(emit)
        return (sem, myseq)

    def wait(self, eng, deps):
        waits = self._waits(eng, deps)
        if not waits:
            return

        def emit(e, sems, waits=tuple(waits)):
            for ws, wv in waits:
                e.wait_ge(sems[ws], self.val(ws, wv))

        self.prog[eng].append(emit)

    def finalize_vals(self):
        self.valmap = {}
        for s, seqs in self.need.items():
            m = {}
            c = 0
            for q in sorted(seqs):
                c += self.kstep[s]
                m[q] = c
            self.valmap[s] = m

    def val(self, sem, seq):
        return self.valmap[sem][seq]


class Buf:
    __slots__ = ("ap", "w", "r")

    def __init__(self, ap):
        self.ap = ap
        self.w = None
        self.r = []

    def war(self):
        d = list(self.r)
        if self.w is not None:
            d.append(self.w)
        return d

    def setw(self, tok):
        self.w = tok
        self.r = []

    def addr(self, tok):
        self.r.append(tok)


def build_program(dev_stage="full"):
    nc = bass.Bass("TRN2", target_bir_lowering=False, debug=False)
    st = ExitStack()
    stacks_to_close = []
    B = Bld()
    dram = {}

    def din(name, shape, dtype=F32):
        dram[name] = nc.dram_tensor(name, list(shape), dtype, kind="ExternalInput")
        return dram[name]

    def dout(name, shape, dtype=F32):
        dram[name] = nc.dram_tensor(name, list(shape), dtype, kind="ExternalOutput")
        return dram[name]

    def dint(name, shape, dtype=F32):
        dram[name] = nc.dram_tensor(name, list(shape), dtype)
        return dram[name]

    def sbuf(name, shape, dtype=F32):
        return st.enter_context(nc.sbuf_tensor(name, list(shape), dtype))

    def gop(eng, fn, reads=(), writes=(), extra=(), sem=None, k=None):
        deps = list(extra)
        for b in reads:
            deps.append(b.w)
        for b in writes:
            deps.extend(b.war())
        tok = B.op(eng, fn, deps=deps, sem=sem, k=k)
        for b in reads:
            b.addr(tok)
        for b in writes:
            b.setw(tok)
        return tok

    def dma(eng, dst_ap, src_ap, reads=(), writes=(), extra=(), q="ld"):
        sem = B.lane(q)
        prev = (sem, B.cnt[sem]) if B.cnt[sem] else None
        return gop(eng, lambda e, s, d=dst_ap, r=src_ap: e.dma_start(out=d, in_=r),
                   reads=reads, writes=writes,
                   extra=tuple(extra) + ((prev,) if prev else ()), sem=sem)

    def mm(out_ap, lhsT_ap, rhs_ap, start, stop, reads=(), extra=()):
        return gop("tensor",
                   lambda e, s, o=out_ap, l=lhsT_ap, r=rhs_ap, a=start, b=stop:
                   e.matmul(o, l, r, start=a, stop=b),
                   reads=reads, extra=extra)

    def V(fname, *args, reads=(), writes=(), extra=(), **kw):
        return gop("vector",
                   lambda e, s, f=fname, a=args, kw=kw: getattr(e, f)(*a, **kw),
                   reads=reads, writes=writes, extra=extra)

    def S(fname, *args, reads=(), writes=(), extra=(), **kw):
        return gop("scalar",
                   lambda e, s, f=fname, a=args, kw=kw: getattr(e, f)(*a, **kw),
                   reads=reads, writes=writes, extra=extra)

    def CCall(name_i, name_o, groups, reads=(), writes=(), extra=()):
        return gop("gpsimd",
                   lambda e, s, bi=dram[name_i], bo=dram[name_o], g=groups:
                   e.collective_compute("AllReduce", OP.add, replica_groups=g,
                                        ins=[bi.ap().opt()], outs=[bo.ap().opt()]),
                   reads=reads, writes=writes,
                   extra=tuple(extra) + (("CC", B.cnt["CC"]) if B.cnt["CC"] else None,),
                   sem="CC", k=1)

    def pgrp_begin(pbuf):
        return tuple(pbuf.war())

    def pgrp_end(pbuf):
        pbuf.setw(("Et", B.cnt["Et"]))

    din("x", [C, T])
    din("l0vw", [C, C]); din("l0rw", [C, C]); din("l0ow", [C, C])
    din("l0g", [C, HID0], I32); din("l0u", [C, HID0], I32); din("l0d", [HID0, C], I32)
    din("l0wd", [128, NCH]); din("l0tf", [128, NCH])
    din("cmask", [128, 1])
    din("iota", [1, T])
    dint("hx1", [C, T])
    dint("hxi", [C, T])
    for ch in range(NCH):
        dint(f"cc1i{ch}", [128, 2])
        dint(f"cc1o{ch}", [128, 2])
    dint("cc2i", [128, 520]); dint("cc2o", [128, 520])
    din("l1aw", [C, 3072]); din("l1awf", [64, 64])
    din("l1abf", [128, 1]); din("l1abfr", [1, 128])
    din("l1awo", [C, C]); din("l1abo", [128, NCH])
    din("idnb", [128, 128], BF16)
    if dev_stage == "l0":
        dout("y_x1", [C, T])
        dout("y_xn", [C, T], BF16)
        dout("y_prod", [C, T], BF16)
    if dev_stage in ("l1a", "l1a3"):
        dout("y_x1p", [C, T])
    for l in ([] if CUT == "nodecl" else range(2)):
        din(f"l1fw{l}", [C, 1536]); din(f"l1fwo{l}", [512, C])
        din(f"l1fw1{l}", [C, 5460]); din(f"l1fw2{l}", [2730, C])
        din(f"l1fgb{l}", [2, C])
        din(f"l1fqs{l}", [1, 128]); din(f"l1fks{l}", [1, 128])
        din(f"l1fb1a{l}", [128, 22]); din(f"l1fb1g{l}", [128, 22])
        din(f"l1fb2{l}", [128, NCH]); din(f"l1fgc{l}", [128, NCH])
        for g in range(4):
            for ab in "ab":
                dint(f"cc3{ab}{l}{g}i", [128, 1536], BF16)
                dint(f"cc3{ab}{l}{g}o", [128, 1536], BF16)
    if CUT != "nodecl":
        for j in range(4):
            for ab in "ab":
                dint(f"ccxe{j}{ab}i", [128, NCH], BF16)
                dint(f"ccxe{j}{ab}o", [128, NCH], BF16)
        din("csq", [128, 1024], BF16); din("ssq", [128, 1024], BF16)
        din("csk", [128, 1024], BF16); din("ssk", [128, 1024], BF16)
        din("mska", [128, 128], BF16); din("mskb", [128, 128], BF16)
        din("mle", [128, 128], BF16); din("mge", [128, 128], BF16)
        din("mskfw", [128, 2048], BF16)
        din("useA", [128, 1]); din("useB", [128, 1])
        din("sendA", [128, 1]); din("sendB", [128, 1])
        din("idnf", [128, 128], F32)
    if dev_stage == "full":
        dout("y", [C, T])
    if dev_stage in ("l1f0", "l1f1"):
        dout("y_xi", [C, T])

    XN = sbuf("XN", [128, NCH * T], BF16)
    WSF = sbuf("WSF", [128, 2 * 512], F32)
    WSI = sbuf("WSI", [128, 2 * 512], I32)
    WB = sbuf("WB", [128, 4 * 512], BF16)
    LNS = sbuf("LNS", [128, T], F32)
    ROWS = sbuf("ROWS", [1, 2 * T], F32)
    XSTR = sbuf("XSTR", [128, 2 * T], F32)
    WS1 = sbuf("WS1", [128, T], F32)
    WS2 = sbuf("WS2", [128, T], F32)
    CARR = sbuf("CARR", [128, NCH * 2], F32)
    CMSK = sbuf("CMSK", [128, 1], F32)
    CSND = sbuf("CSND", [128, 2], F32)
    COEF = sbuf("COEF", [128, 8 * NCH], F32)
    ONE1 = sbuf("ONE1", [128, 1], F32)
    ONE1B = sbuf("ONE1B", [128, 1], BF16)
    ONER = sbuf("ONER", [1, 128], F32)
    CONSTS = sbuf("CONSTS", [128, 3], F32)
    st1b = ExitStack()
    stacks_to_close.insert(0, st1b)
    PR = st1b.enter_context(nc.sbuf_tensor("PR", [128, NCH * T], BF16))
    st1a = ExitStack()
    IOTA = st1a.enter_context(nc.sbuf_tensor("IOTA", [128, T], F32))
    VCH = st1a.enter_context(nc.sbuf_tensor("VCH", [128, 2 * T], F32))
    YAA = st1a.enter_context(nc.sbuf_tensor("YAA", [128, 2 * (T + 1)], F32))
    YBB = st1a.enter_context(nc.sbuf_tensor("YBB", [128, 2 * (T + 1)], F32))
    RCH = st1a.enter_context(nc.sbuf_tensor("RCH", [128, 2 * T], F32))

    P2a = st.enter_context(nc.psum_tensor("P2a", [128, 1024], F32))
    P2b = st.enter_context(nc.psum_tensor("P2b", [128, 1024], F32))
    stP = ExitStack()
    P4 = stP.enter_context(nc.psum_tensor("P4", [128, 2048], F32))

    bXN = [Buf(XN[:, ch * T:(ch + 1) * T]) for ch in range(NCH)]
    bWSF = [Buf(None) for _ in range(2)]
    bWSI = [Buf(None) for _ in range(2)]
    bWB = [Buf(None) for _ in range(4)]
    bLNS = Buf(None)
    bROW1 = Buf(None); bROW2 = Buf(None)
    bXSTR = [Buf(XSTR[:, i * T:(i + 1) * T]) for i in range(2)]
    bVCH = [Buf(VCH[:, i * T:(i + 1) * T]) for i in range(2)]
    bYAA = [Buf(YAA[:, i * (T + 1):(i + 1) * (T + 1)]) for i in range(2)]
    bYBB = [Buf(YBB[:, i * (T + 1):(i + 1) * (T + 1)]) for i in range(2)]
    bRCH = [Buf(RCH[:, i * T:(i + 1) * T]) for i in range(2)]
    bWS1 = Buf(None); bWS2 = Buf(None)
    bCARR = [Buf(None) for _ in range(NCH)]
    bCMSK = Buf(None); bCSND = Buf(None); bIOTA = Buf(None)
    bPR = [Buf(None) for _ in range(NCH)]
    bCOEF = Buf(None); bONE1 = Buf(None); bONE1B = Buf(None); bONER = Buf(None)
    bCONSTS = Buf(None)
    bP4 = Buf(None); bP2a = Buf(None); bP2b = Buf(None)
    bHX1 = [Buf(None) for _ in range(NCH)]
    bHXI = [Buf(None) for _ in range(NCH)]
    bCC1i = [Buf(None) for _ in range(NCH)]
    bCC1o = [Buf(None) for _ in range(NCH)]

    ALPH, BETA, GAMM, DELT, DELT2, NRWC, WDTC, TFTC = range(8)

    def ccol(group, ch):
        return COEF[:, group * NCH + ch:group * NCH + ch + 1]

    def cbc(group, ch, n):
        a = COEF.ap()
        return bass.AP(a.tensor, group * NCH + ch, [[8 * NCH, 128], [0, n]])

    dma("sync", IOTA[:, :], bass.AP(dram["iota"], 0, [[0, 128], [1, T]]),
        writes=(bIOTA,))
    dma("sync", CMSK[:, :], dram["cmask"].ap(), writes=(bCMSK,))
    dma("sync", COEF[:, WDTC * NCH:(WDTC + 1) * NCH], dram["l0wd"].ap(),
        writes=(bCOEF,))
    dma("sync", COEF[:, TFTC * NCH:(TFTC + 1) * NCH], dram["l0tf"].ap(),
        reads=(bCOEF,), writes=(bCOEF,))
    V("memset", ONE1[:, :], 1.0, writes=(bONE1,))
    V("memset", ONE1B[:, :], 1.0, writes=(bONE1B,))
    V("memset", ONER[:, :], 1.0, writes=(bONER,))
    V("memset", CONSTS[:, 0:1], 1e-5, writes=(bCONSTS,))
    V("memset", CONSTS[:, 1:2], 1.0 / 127.0, reads=(bCONSTS,), writes=(bCONSTS,))
    V("memset", CONSTS[:, 2:3], 1e-37, reads=(bCONSTS,), writes=(bCONSTS,))

    cw = COEF[:, WDTC * NCH:(WDTC + 1) * NCH]
    cu = COEF[:, TFTC * NCH:(TFTC + 1) * NCH]
    ca = COEF[:, ALPH * NCH:(ALPH + 1) * NCH]
    cb = COEF[:, BETA * NCH:(BETA + 1) * NCH]
    cg = COEF[:, GAMM * NCH:(GAMM + 1) * NCH]
    cd = COEF[:, DELT * NCH:(DELT + 1) * NCH]
    cd2 = COEF[:, DELT2 * NCH:(DELT2 + 1) * NCH]
    cnr = COEF[:, NRWC * NCH:(NRWC + 1) * NCH]
    S("activation", cnr, cw, AF.Relu, scale=-1.0, reads=(bCOEF,), writes=(bCOEF,))
    S("mul", cnr, cnr, -1.0, reads=(bCOEF,), writes=(bCOEF,))
    S("activation", cnr, cw, AF.Relu, reads=(bCOEF,), writes=(bCOEF,))
    S("mul", cnr, cnr, -1.0, reads=(bCOEF,), writes=(bCOEF,))
    S("activation", ca, cnr, AF.Exp, reads=(bCOEF,), writes=(bCOEF,))
    S("activation", cb, cw, AF.Relu, scale=-1.0, reads=(bCOEF,), writes=(bCOEF,))
    S("activation", cb, cb, AF.Exp, scale=-1.0, reads=(bCOEF,), writes=(bCOEF,))
    S("activation", cg, cu, AF.Relu, reads=(bCOEF,), writes=(bCOEF,))
    S("activation", cg, cg, AF.Exp, scale=-1.0, reads=(bCOEF,), writes=(bCOEF,))
    S("activation", cd, cu, AF.Relu, scale=-1.0, reads=(bCOEF,), writes=(bCOEF,))
    S("activation", cd, cd, AF.Exp, scale=-1.0, reads=(bCOEF,), writes=(bCOEF,))
    V("tensor_scalar", cd2, cd, 1e-6, None, OP.add, reads=(bCOEF,), writes=(bCOEF,))

    def ln_fm_hbm(src_name, dst_bufs, dst_ap_fn, src_bufs=None, f32_out=None):
        seq = [0]

        def load(ch):
            sl = seq[0] % 2
            seq[0] += 1
            b = bXSTR[sl]
            dma("sync", b.ap, dram[src_name].ap()[ch * 128:(ch + 1) * 128, :],
                reads=(src_bufs[ch],) if src_bufs else (), writes=(b,))
            return b

        p4w = pgrp_begin(bP4)
        p2aw = pgrp_begin(bP2a)
        p2bw = pgrp_begin(bP2b)
        for ch in range(NCH):
            bx = load(ch)
            xb_ap = dst_ap_fn(ch)
            S("copy", xb_ap, bx.ap, reads=(bx,), writes=(dst_bufs[ch],))
            V("tensor_tensor", LNS[:, :], xb_ap, xb_ap, OP.mult,
              reads=(dst_bufs[ch],), writes=(bLNS,))
            for tt in range(4):
                mm(P4.ap()[0:1, tt * 512:(tt + 1) * 512], ONE1B[:, 0:1],
                   xb_ap[:, tt * 512:(tt + 1) * 512], ch == 0, ch == NCH - 1,
                   reads=(bONE1B, dst_bufs[ch]), extra=p4w)
            for tt in range(4):
                dst = (P2a if tt < 2 else P2b).ap()[0:1,
                                                    (tt % 2) * 512:(tt % 2) * 512 + 512]
                mm(dst, ONE1[:, 0:1], LNS[:, tt * 512:(tt + 1) * 512],
                   ch == 0, ch == NCH - 1, reads=(bONE1, bLNS),
                   extra=p2aw if tt < 2 else p2bw)
        pgrp_end(bP4); pgrp_end(bP2a); pgrp_end(bP2b)
        V("tensor_scalar", ROWS[0:1, 0:T], P4.ap()[0:1, 0:T], 1.0 / C, None,
          OP.mult, reads=(bP4,), writes=(bROW1,))
        V("tensor_scalar", ROWS[0:1, T:T + 1024], P2a.ap()[0:1, :], 1.0 / C, None,
          OP.mult, reads=(bP2a,), writes=(bROW2,))
        V("tensor_scalar", ROWS[0:1, T + 1024:T + 2048], P2b.ap()[0:1, :], 1.0 / C, None,
          OP.mult, reads=(bP2b,), writes=(bROW2,))
        V("tensor_tensor", LNS[0:1, 0:T], ROWS[0:1, 0:T], ROWS[0:1, 0:T], OP.mult,
          reads=(bROW1,), writes=(bLNS,))
        V("tensor_tensor", ROWS[0:1, T:2 * T], ROWS[0:1, T:2 * T], LNS[0:1, 0:T],
          OP.subtract, reads=(bROW2, bLNS), writes=(bROW2,))
        S("activation", LNS[0:1, 0:T], ROWS[0:1, T:2 * T], AF.Ln,
          bias=CONSTS[0:1, 0:1], reads=(bROW2, bCONSTS), writes=(bLNS,))
        S("activation", ROWS[0:1, T:2 * T], LNS[0:1, 0:T], AF.Exp, scale=-0.5,
          reads=(bLNS,), writes=(bROW2,))
        V("tensor_tensor", ROWS[0:1, 0:T], ROWS[0:1, 0:T], ROWS[0:1, T:2 * T], OP.mult,
          reads=(bROW2, bROW1), writes=(bROW1,))
        for hf in range(2):
            p2aw = pgrp_begin(bP2a); p2bw = pgrp_begin(bP2b)
            for q in range(2):
                mm(P2a.ap()[:, q * 512:(q + 1) * 512], ONER[0:1, :],
                   ROWS[0:1, T + hf * 1024 + q * 512:T + hf * 1024 + (q + 1) * 512],
                   True, True, reads=(bONER, bROW2), extra=p2aw)
                mm(P2b.ap()[:, q * 512:(q + 1) * 512], ONER[0:1, :],
                   ROWS[0:1, hf * 1024 + q * 512:hf * 1024 + (q + 1) * 512],
                   True, True, reads=(bONER, bROW1), extra=p2bw)
            pgrp_end(bP2a); pgrp_end(bP2b)
            for ch in range(NCH):
                bx = load(ch)
                sl = slice(hf * 1024, hf * 1024 + 1024)
                V("tensor_tensor", LNS[:, 0:1024], bx.ap[:, sl], P2a.ap()[:, :],
                  OP.mult, reads=(bx, bP2a), writes=(bLNS,))
                V("tensor_tensor", dst_ap_fn(ch)[:, sl], LNS[:, 0:1024],
                  P2b.ap()[:, :], OP.subtract, reads=(bLNS, bP2b),
                  writes=(dst_bufs[ch],))
                if f32_out is not None:
                    fname, fbufs = f32_out
                    V("tensor_tensor", WS2[:, 0:1024], LNS[:, 0:1024],
                      P2b.ap()[:, :], OP.subtract, reads=(bLNS, bP2b),
                      writes=(bWS2,))
                    dma("gpsimd",
                        dram[fname].ap()[ch * 128:(ch + 1) * 128, sl],
                        WS2[:, 0:1024], reads=(bWS2,), writes=(fbufs[ch],),
                        q="gp")

    ln_fm_hbm("x", bXN, lambda ch: XN[:, ch * T:(ch + 1) * T])

    wseq = [0]

    def wtile(src, k0, kparts, m0, mt, int_src=False):
        n = wseq[0]; wseq[0] += 1
        fb = (bWSI if int_src else bWSF)[n % 2]
        wb = bWB[n % 4]
        stage_t = WSI if int_src else WSF
        src_ap = src.ap()[k0:k0 + kparts, m0:m0 + mt]
        stage_ap = stage_t[0:kparts, (n % 2) * 512:(n % 2) * 512 + mt]
        dma("sync", stage_ap, src_ap, writes=(fb,))
        dst_ap = WB[0:kparts, (n % 4) * 512:(n % 4) * 512 + mt]
        S("copy", dst_ap, stage_ap, reads=(fb,), writes=(wb,))
        return dst_ap, wb

    def proj_fm(wname, m0, evac, int_src=False, rhs_fn=None, rbufs=None,
                nkc=NCH):
        if rhs_fn is None:
            rhs_fn = lambda kc, lo, hi: XN[:, kc * T + lo:kc * T + hi]
            rbufs = bXN
        for hf, (PT, PB) in enumerate(((P2a, bP2a), (P2b, bP2b))):
            pw = pgrp_begin(PB)
            for kc in range(nkc):
                w_ap, wb = wtile(dram[wname], kc * 128, 128, m0, 128,
                                 int_src=int_src)
                for q in range(2):
                    mm(PT.ap()[:, q * 512:(q + 1) * 512], w_ap,
                       rhs_fn(kc, hf * 1024 + q * 512, hf * 1024 + (q + 1) * 512),
                       kc == 0, kc == nkc - 1, reads=(wb, rbufs[kc]), extra=pw)
            pgrp_end(PB)
            evac(hf, PT, PB)

    def l0_scan_part(ch):
        sl = ch % 2
        bv, bya, byb, br = bVCH[sl], bYAA[sl], bYBB[sl], bRCH[sl]

        def evac_v(hf, PT, PB):
            S("copy", bv.ap[:, hf * 1024:(hf + 1) * 1024], PT.ap()[:, :],
              reads=(PB,), writes=(bv,))
        proj_fm("l0vw", ch * 128, evac_v)

        V("tensor_scalar", WS1[:, :], bv.ap, ccol(BETA, ch), None, OP.mult,
          reads=(bv, bCOEF), writes=(bWS1,))
        V("memset", bya.ap[:, 0:1], 0.0, writes=(bya,))
        V("tensor_tensor_scan", bya.ap[:, 1:T + 1], cbc(ALPH, ch, T), WS1[:, :],
          0.0, OP.mult, OP.add, reads=(bCOEF, bWS1), writes=(bya,))
        V("memset", byb.ap[:, 0:1], 0.0, writes=(byb,))
        V("tensor_tensor_scan", byb.ap[:, 1:T + 1], cbc(ALPH, ch, T),
          cbc(BETA, ch, T), 0.0, OP.mult, OP.add, reads=(bCOEF,), writes=(byb,))
        V("scalar_tensor_tensor", WS2[:, 0:1], bya.ap[:, T:T + 1], CMSK[:, :],
          bya.ap[:, T:T + 1], OP.mult, OP.subtract, reads=(bya, bCMSK),
          writes=(bWS2,))
        V("scalar_tensor_tensor", WS2[:, 1:2], byb.ap[:, T:T + 1], CMSK[:, :],
          byb.ap[:, T:T + 1], OP.mult, OP.subtract, reads=(byb, bCMSK),
          writes=(bWS2,))
        V("tensor_scalar", CSND[:, 0:2], WS2[:, 0:2], -1.0, None, OP.mult,
          reads=(bWS2,), writes=(bCSND,))
        dma("gpsimd", dram[f"cc1i{ch}"].ap(), CSND[:, :], reads=(bCSND,),
            writes=(bCC1i[ch],), q="gp")
        CCall(f"cc1i{ch}", f"cc1o{ch}", PAIRS, reads=(bCC1i[ch],),
              writes=(bCC1o[ch],))
        dma("gpsimd", CARR[:, 2 * ch:2 * ch + 2], dram[f"cc1o{ch}"].ap(),
            reads=(bCC1o[ch],), writes=(bCARR[ch],), q="gp")

        def evac_r(hf, PT, PB):
            S("activation", br.ap[:, hf * 1024:(hf + 1) * 1024], PT.ap()[:, :],
              AF.Sigmoid, reads=(PB,), writes=(br,))
        proj_fm("l0rw", ch * 128, evac_r)

    def l0_wkv_part(ch):
        sl = ch % 2
        bv, bya, byb, br = bVCH[sl], bYAA[sl], bYBB[sl], bRCH[sl]
        V("tensor_scalar", WS2[:, 0:2], CARR[:, 2 * ch:2 * ch + 2], CMSK[:, :],
          None, OP.mult, reads=(bCARR[ch], bCMSK), writes=(bWS2,))
        S("activation", LNS[:, :], IOTA[:, :], AF.Exp, scale=ccol(NRWC, ch),
          reads=(bIOTA, bCOEF), writes=(bLNS,))
        V("scalar_tensor_tensor", bya.ap[:, 0:T], LNS[:, :], WS2[:, 0:1],
          bya.ap[:, 0:T], OP.mult, OP.add, reads=(bLNS, bWS2, bya), writes=(bya,))
        V("scalar_tensor_tensor", byb.ap[:, 0:T], LNS[:, :], WS2[:, 1:2],
          byb.ap[:, 0:T], OP.mult, OP.add, reads=(bLNS, bWS2, byb), writes=(byb,))
        V("tensor_scalar", WS1[:, :], bv.ap, ccol(DELT, ch), None, OP.mult,
          reads=(bv, bCOEF), writes=(bWS1,))
        V("scalar_tensor_tensor", WS1[:, :], bya.ap[:, 0:T], ccol(GAMM, ch),
          WS1[:, :], OP.mult, OP.add, reads=(bya, bCOEF, bWS1), writes=(bWS1,))
        V("tensor_scalar", LNS[:, :], byb.ap[:, 0:T], ccol(GAMM, ch),
          ccol(DELT2, ch), OP.mult, OP.add, reads=(byb, bCOEF), writes=(bLNS,))
        S("activation", LNS[:, :], LNS[:, :], AF.Ln, reads=(bLNS,), writes=(bLNS,))
        S("activation", LNS[:, :], LNS[:, :], AF.Exp, scale=-1.0, reads=(bLNS,),
          writes=(bLNS,))
        V("tensor_tensor", WS1[:, :], WS1[:, :], LNS[:, :], OP.mult,
          reads=(bWS1, bLNS), writes=(bWS1,))
        V("tensor_tensor", PR[:, ch * T:(ch + 1) * T], WS1[:, :], br.ap, OP.mult,
          reads=(bWS1, br), writes=(bPR[ch],))

    for ch in range(NCH):
        l0_scan_part(ch)
        if ch >= 1:
            l0_wkv_part(ch - 1)
    l0_wkv_part(NCH - 1)

    alias_deps = []
    for b in (bVCH + bYAA + bYBB + bRCH + [bIOTA]):
        alias_deps.extend(b.war())
    st1a.close()
    HFF = st1b.enter_context(nc.sbuf_tensor("HFF", [128, 24 * 1024], BF16))
    GSI = st1b.enter_context(nc.sbuf_tensor("GSI", [128, 1024], BF16))
    GUP = st1b.enter_context(nc.sbuf_tensor("GUP", [128, 1024], BF16))
    bHFF = [Buf(None) for _ in range(24)]
    bGSI = Buf(None); bGUP = Buf(None)
    for b in bHFF + [bGSI, bGUP]:
        b.r = list(alias_deps)

    S127 = 1.0 / 127.0
    for hf in range(2):
        for m in range(24):
            pga = pgrp_begin(bP2a); pgb = pgrp_begin(bP2b)
            for kc in range(NCH):
                wg_ap, wgb = wtile(dram["l0g"], kc * 128, 128, m * 128, 128,
                                   int_src=True)
                wu_ap, wub = wtile(dram["l0u"], kc * 128, 128, m * 128, 128,
                                   int_src=True)
                for q in range(2):
                    rhs = XN[:, kc * T + hf * 1024 + q * 512:
                             kc * T + hf * 1024 + (q + 1) * 512]
                    mm(P2a.ap()[:, q * 512:(q + 1) * 512], wg_ap, rhs,
                       kc == 0, kc == NCH - 1, reads=(wgb, bXN[kc]), extra=pga)
                    mm(P2b.ap()[:, q * 512:(q + 1) * 512], wu_ap, rhs,
                       kc == 0, kc == NCH - 1, reads=(wub, bXN[kc]), extra=pgb)
            pgrp_end(bP2a); pgrp_end(bP2b)
            S("activation", GSI[:, :], P2a.ap()[:, :], AF.Silu,
              scale=CONSTS[:, 1:2], reads=(bP2a, bCONSTS), writes=(bGSI,))
            S("copy", GUP[:, :], P2b.ap()[:, :], reads=(bP2b,), writes=(bGUP,))
            V("tensor_tensor", HFF[:, m * 1024:(m + 1) * 1024], GSI[:, :],
              GUP[:, :], OP.mult, reads=(bGSI, bGUP), writes=(bHFF[m],))
        for och in range(NCH):
            pga = pgrp_begin(bP2a)
            for kc in range(NCH):
                w_ap, wb = wtile(dram["l0ow"], kc * 128, 128, och * 128, 128)
                for q in range(2):
                    mm(P2a.ap()[:, q * 512:(q + 1) * 512], w_ap,
                       PR[:, kc * T + hf * 1024 + q * 512:
                          kc * T + hf * 1024 + (q + 1) * 512],
                       kc == 0, kc == NCH - 1, reads=(wb, bPR[kc]), extra=pga)
            pgrp_end(bP2a)
            bx = bXSTR[och % 2]
            dma("sync", bx.ap[:, 0:1024],
                dram["x"].ap()[och * 128:(och + 1) * 128,
                               hf * 1024:(hf + 1) * 1024], writes=(bx,))
            V("tensor_tensor", LNS[:, 0:1024], bx.ap[:, 0:1024], P2a.ap()[:, :],
              OP.add, reads=(bx, bP2a), writes=(bLNS,))
            pgb = pgrp_begin(bP2b)
            for kc in range(24):
                w_ap, wb = wtile(dram["l0d"], kc * 128, 128, och * 128, 128,
                                 int_src=True)
                for q in range(2):
                    mm(P2b.ap()[:, q * 512:(q + 1) * 512], w_ap,
                       HFF[:, kc * 1024 + q * 512:kc * 1024 + (q + 1) * 512],
                       kc == 0, kc == 23, reads=(wb, bHFF[kc]), extra=pgb)
            pgrp_end(bP2b)
            V("scalar_tensor_tensor", WS1[:, 0:1024], P2b.ap()[:, :],
              S127 * S127, LNS[:, 0:1024], OP.mult, OP.add,
              reads=(bP2b, bLNS), writes=(bWS1,))
            dma("gpsimd", dram["hx1"].ap()[och * 128:(och + 1) * 128,
                                           hf * 1024:(hf + 1) * 1024],
                WS1[:, 0:1024], reads=(bWS1,), writes=(bHX1[och],), q="gp")
            if dev_stage == "l0":
                dma("gpsimd", dram["y_x1"].ap()[och * 128:(och + 1) * 128,
                                                hf * 1024:(hf + 1) * 1024],
                    WS1[:, 0:1024], reads=(bWS1,), q="out")

    if dev_stage == "l0":
        for ch in range(NCH):
            dma("gpsimd", dram["y_xn"].ap()[ch * 128:(ch + 1) * 128, :],
                XN[:, ch * T:(ch + 1) * T], reads=(bXN[ch],), q="out")
            dma("gpsimd", dram["y_prod"].ap()[ch * 128:(ch + 1) * 128, :],
                PR[:, ch * T:(ch + 1) * T], reads=(bPR[ch],), q="out")

    if dev_stage not in ("l0", "l1a"):
        pass
    if dev_stage != "l0":
        alias2 = []
        for b in bPR + bHFF + [bGSI, bGUP]:
            alias2.extend(b.war())
        st1b.close()
        st2 = ExitStack()
        stacks_to_close.append(st2)

        def sb2(name, shape, dtype=BF16):
            return st2.enter_context(nc.sbuf_tensor(name, list(shape), dtype))

        QF = sb2("QF", [128, T]); KF = sb2("KF", [128, T])
        QP = sb2("QP", [128, T]); KP = sb2("KP", [128, T], F32)
        VF = sb2("VF", [128, T], F32)
        VAU = sb2("VAU", [128, 16 * 130])
        KVP = sb2("KVP", [128, 520], F32)
        KVF = sb2("KVF", [128, 520], F32)
        BDK = sb2("BDK", [128, 130])
        ONRM = sb2("ONRM", [128, 16 * 128], F32)
        AOF = sb2("AOF", [128, NCH * T])
        WFBD = sb2("WFBD", [128, 128])
        WFS = sb2("WFS", [64, 64], F32)
        WF1 = sb2("WF1", [64, 64])
        BFC = sb2("BFC", [128, 1], F32)
        BFR = sb2("BFR", [1, 128], F32)
        BOC = sb2("BOC", [128, NCH], F32)
        IDN = sb2("IDN", [128, 128], F32)
        RCD = sb2("RCD", [128, 2], F32)
        ONERB = sb2("ONERB", [1, 128])
        names2 = ["QF", "KF", "QP", "KP", "VF", "VAU", "KVP", "KVF", "BDK",
                  "ONRM", "AOF", "WFBD", "WFS", "WF1", "BFC", "BFR", "BOC",
                  "IDN", "RCD", "ONERB"]
        b2 = {}
        for n in names2:
            b2[n] = Buf(None)
            b2[n].r = list(alias2)
        bAOF = [Buf(None) for _ in range(NCH)]
        for b in bAOF:
            b.r = list(alias2)
        bP4b = [Buf(None) for _ in range(4)]

        V("memset", WFBD[:, :], 0.0, writes=(b2["WFBD"],))
        dma("gpsimd", WFBD[0:64, 0:64], dram["l1awf"].ap(),
            reads=(), writes=(b2["WFBD"],), q="gp")
        dma("gpsimd", WFBD[64:128, 64:128], dram["l1awf"].ap(),
            reads=(), writes=(b2["WFBD"],), q="gp")
        dma("sync", BFC[:, :], dram["l1abf"].ap(), writes=(b2["BFC"],))
        dma("sync", BFR[:, :], dram["l1abfr"].ap(), writes=(b2["BFR"],))
        dma("sync", BOC[:, :], dram["l1abo"].ap(), writes=(b2["BOC"],))
        dma("sync", IDN[:, :], dram["idnf"].ap(), writes=(b2["IDN"],))
        V("memset", ONERB[:, :], 1.0, writes=(b2["ONERB"],))
        vau_t = VAU.ap().tensor
        V("memset", bass.AP(vau_t, 64, [[16 * 130, 128], [130, 16], [1, 1]]),
          1.0, reads=(b2["VAU"],), writes=(b2["VAU"],))
        V("memset", bass.AP(vau_t, 129, [[16 * 130, 128], [130, 16], [1, 1]]),
          1.0, reads=(b2["VAU"],), writes=(b2["VAU"],))

        ln_fm_hbm("hx1", bXN, lambda ch: XN[:, ch * T:(ch + 1) * T],
                  src_bufs=bHX1, f32_out=("hxi", bHXI))

        if CUT == "ln2":
            B.wait("gpsimd", [(s, B.cnt[s]) for v in B.pools.values() for s in v
                              if B.cnt[s]])
            B.wait("gpsimd", [("Et", B.cnt["Et"]), ("Ev", B.cnt["Ev"]),
                              ("Es", B.cnt["Es"])])
            return _finish(nc, st, stP, stacks_to_close, B)
        for b in bP4b:
            b.r = list(bP4.war())

        def transpose_tile(src_ap, src_buf, dst_ap, dst_buf, tc):
            bk = bP4b[tc % 4]
            pslice = P4.ap()[:, (tc % 4) * 512:(tc % 4) * 512 + 128]
            tt = gop("tensor",
                     lambda e, s, o=pslice, i=src_ap, d=IDN[:, :]:
                     e.transpose(o, i, d),
                     reads=(src_buf, b2["IDN"]), writes=(bk,))
            V("tensor_copy", dst_ap, pslice, reads=(bk,), writes=(dst_buf,))

        def fmap_fm(srcT, src_b, dstT, dst_b):
            for hf, (PT, PB) in enumerate(((P2a, bP2a), (P2b, bP2b))):
                pw = pgrp_begin(PB)
                for q2 in range(2):
                    mm(PT.ap()[:, q2 * 512:(q2 + 1) * 512], WFBD[:, :],
                       srcT[:, hf * 1024 + q2 * 512:hf * 1024 + (q2 + 1) * 512],
                       True, True, reads=(b2["WFBD"], src_b), extra=pw)
                pgrp_end(PB)
                S("activation", dstT[:, hf * 1024:(hf + 1) * 1024],
                  PT.ap()[:, :], AF.Relu, bias=BFC[:, :],
                  reads=(PB, b2["BFC"]), writes=(dst_b,))

        def _cut(tag):
            if CUT == tag:
                B.wait("gpsimd", [(s, B.cnt[s]) for v in B.pools.values()
                                  for s in v if B.cnt[s]])
                B.wait("gpsimd", [("Et", B.cnt["Et"]), ("Ev", B.cnt["Ev"]),
                                  ("Es", B.cnt["Es"])])
                return True
            return False

        for g in range(8):
            def evk(hf, PT, PB):
                S("copy", KF[:, hf * 1024:(hf + 1) * 1024], PT.ap()[:, :],
                  reads=(PB,), writes=(b2["KF"],))
            proj_fm("l1aw", 1024 + g * 128, evk)

            def evv(hf, PT, PB):
                S("copy", VF[:, hf * 1024:(hf + 1) * 1024], PT.ap()[:, :],
                  reads=(PB,), writes=(b2["VF"],))
            proj_fm("l1aw", 2048 + g * 128, evv)

            if _cut(f"g{g}k"):
                return _finish(nc, st, stP, stacks_to_close, B)
            fmap_fm(KF, b2["KF"], KP, b2["KP"])
            if _cut(f"g{g}f"):
                return _finish(nc, st, stP, stacks_to_close, B)
            for tc in range(16):
                transpose_tile(KP[:, tc * 128:(tc + 1) * 128], b2["KP"],
                               KF[:, tc * 128:(tc + 1) * 128], b2["KF"], tc)
            for tc in range(16):
                bk = bP4b[tc % 4]
                pslice = P4.ap()[:, (tc % 4) * 512:(tc % 4) * 512 + 128]
                gop("tensor",
                    lambda e, s, o=pslice, i=VF[:, tc * 128:(tc + 1) * 128],
                    d=IDN[:, :]: e.transpose(o, i, d),
                    reads=(b2["VF"], b2["IDN"]), writes=(bk,))
                V("tensor_copy", VAU[:, tc * 130:tc * 130 + 64], pslice[:, 0:64],
                  reads=(bk,), writes=(b2["VAU"],))
                V("tensor_copy", VAU[:, tc * 130 + 65:tc * 130 + 129],
                  pslice[:, 64:128], reads=(bk,), writes=(b2["VAU"],))

            if _cut(f"g{g}t"):
                return _finish(nc, st, stP, stacks_to_close, B)
            pw = pgrp_begin(bP2a)
            for tc in range(16):
                mm(P2a.ap()[0:64, 0:65], KF[:, tc * 128:tc * 128 + 64],
                   VAU[:, tc * 130:tc * 130 + 65], tc == 0, tc == 15,
                   reads=(b2["KF"], b2["VAU"]), extra=pw)
                mm(P2a.ap()[64:128, 512:577], KF[:, tc * 128 + 64:tc * 128 + 128],
                   VAU[:, tc * 130 + 65:tc * 130 + 130], tc == 0, tc == 15,
                   reads=(b2["KF"], b2["VAU"]), extra=pw)
            pgrp_end(bP2a)
            V("tensor_copy", KVP[0:64, g * 65:(g + 1) * 65], P2a.ap()[0:64, 0:65],
              reads=(bP2a,), writes=(b2["KVP"],))
            V("tensor_copy", KVP[64:128, g * 65:(g + 1) * 65],
              P2a.ap()[64:128, 512:577], reads=(bP2a,), writes=(b2["KVP"],))
            if _cut(f"g{g}kv"):
                return _finish(nc, st, stP, stacks_to_close, B)

        if CUT == "pa":
            B.wait("gpsimd", [(s, B.cnt[s]) for v in B.pools.values() for s in v
                              if B.cnt[s]])
            B.wait("gpsimd", [("Et", B.cnt["Et"]), ("Ev", B.cnt["Ev"]),
                              ("Es", B.cnt["Es"])])
            return _finish(nc, st, stP, stacks_to_close, B)
        bCC2i = Buf(None); bCC2o = Buf(None)
        dma("gpsimd", dram["cc2i"].ap(), KVP[:, :], reads=(b2["KVP"],),
            writes=(bCC2i,), q="gp")
        CCall("cc2i", "cc2o", PAIRS, reads=(bCC2i,), writes=(bCC2o,))
        dma("gpsimd", KVF[:, :], dram["cc2o"].ap(), reads=(bCC2o,),
            writes=(b2["KVF"],), q="gp")
        if CUT == "x2":
            B.wait("gpsimd", [(s, B.cnt[s]) for v in B.pools.values() for s in v
                              if B.cnt[s]])
            B.wait("gpsimd", [("Et", B.cnt["Et"]), ("Ev", B.cnt["Ev"]),
                              ("Es", B.cnt["Es"])])
            return _finish(nc, st, stP, stacks_to_close, B)

        for g in range(8):
            def evq(hf, PT, PB):
                S("copy", QF[:, hf * 1024:(hf + 1) * 1024], PT.ap()[:, :],
                  reads=(PB,), writes=(b2["QF"],))
            proj_fm("l1aw", g * 128, evq)
            fmap_fm(QF, b2["QF"], QP, b2["QP"])
            if CUT == f"pb{g}q":
                B.wait("gpsimd", [(s, B.cnt[s]) for v in B.pools.values()
                                  for s in v if B.cnt[s]])
                return _finish(nc, st, stP, stacks_to_close, B)
            V("memset", BDK[:, :], 0.0, writes=(b2["BDK"],))
            V("tensor_copy", BDK[0:64, 0:65], KVF[0:64, g * 65:(g + 1) * 65],
              reads=(b2["KVF"],), writes=(b2["BDK"],))
            V("tensor_copy", BDK[64:128, 65:130], KVF[64:128, g * 65:(g + 1) * 65],
              reads=(b2["KVF"],), writes=(b2["BDK"],))
            for tc in range(16):
                PT, PB = (P2a, bP2a) if tc % 2 else (P2b, bP2b)
                pw = pgrp_begin(PB)
                mm(PT.ap()[:, 0:130], QP[:, tc * 128:(tc + 1) * 128],
                   BDK[:, :], True, True, reads=(b2["QP"], b2["BDK"]), extra=pw)
                pgrp_end(PB)
                V("tensor_scalar", RCD[:, 0:1], PT.ap()[:, 64:65], 1e-6, None,
                  OP.add, reads=(PB,), writes=(b2["RCD"],))
                V("tensor_scalar", RCD[:, 1:2], PT.ap()[:, 129:130], 1e-6, None,
                  OP.add, reads=(PB, b2["RCD"]), writes=(b2["RCD"],))
                V("reciprocal", RCD[:, 0:2], RCD[:, 0:2], reads=(b2["RCD"],),
                  writes=(b2["RCD"],))
                V("tensor_scalar", ONRM[:, tc * 128:tc * 128 + 64],
                  PT.ap()[:, 0:64], RCD[:, 0:1], None, OP.mult,
                  reads=(PB, b2["RCD"]), writes=(b2["ONRM"],))
                V("tensor_scalar", ONRM[:, tc * 128 + 64:tc * 128 + 128],
                  PT.ap()[:, 65:129], RCD[:, 1:2], None, OP.mult,
                  reads=(PB, b2["RCD"]), writes=(b2["ONRM"],))
            if CUT == f"pb{g}o":
                B.wait("gpsimd", [(s, B.cnt[s]) for v in B.pools.values()
                                  for s in v if B.cnt[s]])
                return _finish(nc, st, stP, stacks_to_close, B)
            for tc in range(16):
                transpose_tile(ONRM[:, tc * 128:(tc + 1) * 128], b2["ONRM"],
                               AOF[:, g * T + tc * 128:g * T + (tc + 1) * 128],
                               bAOF[g], tc)
            if CUT == f"pb{g}t":
                B.wait("gpsimd", [(s, B.cnt[s]) for v in B.pools.values()
                                  for s in v if B.cnt[s]])
                return _finish(nc, st, stP, stacks_to_close, B)

        for och in range(NCH):
            def evo(hf, PT, PB, och=och):
                bx = bXSTR[och % 2]
                dma("sync", bx.ap[:, 0:1024],
                    dram["hx1"].ap()[och * 128:(och + 1) * 128,
                                     hf * 1024:(hf + 1) * 1024],
                    reads=(bHX1[och],), writes=(bx,))
                V("scalar_tensor_tensor", WS1[:, 0:1024], PT.ap()[:, :],
                  BOC[:, och:och + 1], bx.ap[:, 0:1024], OP.add, OP.add,
                  reads=(PB, b2["BOC"], bx), writes=(bWS1,))
                dma("gpsimd", dram["hx1"].ap()[och * 128:(och + 1) * 128,
                                               hf * 1024:(hf + 1) * 1024],
                    WS1[:, 0:1024], reads=(bWS1,), writes=(bHX1[och],), q="gp")
                if dev_stage in ("l1a", "l1a3"):
                    dma("gpsimd", dram["y_x1p"].ap()[och * 128:(och + 1) * 128,
                                                     hf * 1024:(hf + 1) * 1024],
                        WS1[:, 0:1024], reads=(bWS1,), q="out")
            proj_fm("l1awo", och * 128, evo,
                    rhs_fn=lambda kc, lo, hi: AOF[:, kc * T + lo:kc * T + hi],
                    rbufs=bAOF)
            if CUT == f"wo{och}":
                B.wait("gpsimd", [(s, B.cnt[s]) for v in B.pools.values()
                                  for s in v if B.cnt[s]])
                return _finish(nc, st, stP, stacks_to_close, B)


    if dev_stage not in ("l0", "l1a", "l1a3x"):
        alias3 = []
        for n in names2:
            alias3.extend(b2[n].war())
        for b in bAOF:
            alias3.extend(b.war())
        st2.close()
        st3 = ExitStack()
        stacks_to_close.append(st3)

        def sb3(name, shape, dtype=BF16, aliased=True):
            t = st3.enter_context(nc.sbuf_tensor(name, list(shape), dtype))
            bu = Buf(None)
            if aliased:
                bu.r = list(alias3)
            return t, bu

        aliasP2 = []
        for b in bP4b:
            aliasP2.extend(b.war())
        stP.close()
        P4 = stP.enter_context(nc.psum_tensor("P4b", [128, 2048], F32))
        bP4 = Buf(None)
        bP4.r = list(aliasP2)
        bP4h = [Buf(None), Buf(None)]
        for b in bP4h:
            b.r = list(aliasP2)

        XBF, _bxbf = sb3("XBF", [128, NCH * (T + 1)])
        bXBF = [Buf(None) for _ in range(NCH)]
        for b in bXBF:
            b.r = list(alias3)
        OFM, _bofm = sb3("OFM", [128, 4 * T])
        bOFM = [Buf(None) for _ in range(4)]
        for b in bOFM:
            b.r = list(alias3)
        XLS, bXLS = sb3("XLS", [128, NCH])
        XLH, bXLH = sb3("XLH", [128, NCH])
        USEA, bUSEA = sb3("USEA", [128, 1], F32)
        USEB, bUSEB = sb3("USEB", [128, 1], F32)
        SNDA, bSNDA = sb3("SNDA", [128, 1], F32)
        SNDB, bSNDB = sb3("SNDB", [128, 1], F32)
        NRM, bNRM = sb3("NRM", [128, 32], F32)
        DEN, bDEN = sb3("DEN", [128, 4], F32)
        B2S, bB2S = sb3("B2S", [128, 32], F32)
        if CUT != "p30nd":
            for ap_, t_, b_ in ((dram["useA"], USEA, bUSEA),
                                (dram["useB"], USEB, bUSEB),
                                (dram["sendA"], SNDA, bSNDA),
                                (dram["sendB"], SNDB, bSNDB)):
                dma("sync", t_[:, :], ap_.ap(), writes=(b_,))

        sub = {"st": None, "names": None, "freed": list(alias3)}

        def open_attn(l):
            assert sub["st"] is None
            sub["st"] = ExitStack()
            als = sub.pop("freed", [])

            def sba(name, shape, dtype=BF16):
                t = sub["st"].enter_context(
                    nc.sbuf_tensor(f"{name}_{l}", list(shape), dtype))
                bu = Buf(None)
                bu.r = list(als)
                sub["names"][name] = (t, bu)
                return t, bu

            sub["names"] = {}
            for nm, shp, dt_ in (("QT", [128, 2048], BF16), ("KT", [128, 2048], BF16),
                                 ("VT", [128, 2048], BF16),
                                 ("QRA", [65, 2048], BF16), ("QRB", [65, 2048], BF16),
                                 ("KRA", [65, 2560], BF16), ("KRB", [65, 2560], BF16),
                                 ("VHAL", [128, 512], BF16),
                                 ("IDN3", [128, 128], BF16), ("IDF", [128, 128], F32),
                                 ("ATT", [128, 5 * 640], BF16),
                                 ("CSQ", [128, 1024], BF16), ("SSQ", [128, 1024], BF16),
                                 ("CSK", [128, 1024], BF16), ("SSK", [128, 1024], BF16),
                                 ("QSR", [128, 128], BF16), ("KSR", [128, 128], BF16),
                                 ("MSA", [128, 128], BF16), ("MSB", [128, 128], BF16),
                                 ("MLE", [128, 128], BF16), ("MGE", [128, 128], BF16),
                                 ("MFW", [128, 2048], BF16),
                                 ("CH3", [128, 1536], BF16), ("CHM", [128, 1536], BF16)):
                sba(nm, shp, dt_)
            for nm, srcn in (("CSQ", "csq"), ("SSQ", "ssq"), ("CSK", "csk"),
                             ("SSK", "ssk"), ("MSA", "mska"), ("MSB", "mskb"),
                             ("MLE", "mle"), ("MGE", "mge"), ("IDN3", "idnb"),
                             ("IDF", "idnf")):
                t_, b_ = sub["names"][nm]
                dma("sync", t_[:, :], dram[srcn].ap(), writes=(b_,))
            t_, b_ = sub["names"]["MFW"]
            dma("sync", t_[:, :], dram["mskfw"].ap(), writes=(b_,))
            for nm, srcn in (("QSR", f"l1fqs{l}"), ("KSR", f"l1fks{l}")):
                t_, b_ = sub["names"][nm]
                dma("gpsimd", t_[:, :],
                    bass.AP(dram[srcn], 0, [[0, 128], [1, 128]]), writes=(b_,),
                    q="gp")
            ka, bka = sub["names"]["KRA"]
            kb, bkb = sub["names"]["KRB"]
            V("memset", ka[64:65, :], 1.0, writes=(bka,))
            V("memset", kb[64:65, :], 1.0, writes=(bkb,))
            ch3, bch3 = sub["names"]["CH3"]
            V("memset", ch3[64:128, 0:1024], 0.0, writes=(bch3,))
            return sub["names"]

        def close_sub():
            freed = []
            for t_, b_ in sub["names"].values():
                freed.extend(b_.war())
            sub["st"].close()
            sub["st"] = None
            sub["freed"] = freed

        def open_ffn(l):
            assert sub["st"] is None
            sub["st"] = ExitStack()
            als = sub.pop("freed", [])
            names = {}
            for nm, shp, dt_ in (("HFB", [128, 22 * 1024], BF16),
                                 ("GBUF", [128, 1024], BF16),
                                 ("GBR", [2, 1024], BF16), ("GBL", [2, C], BF16),
                                 ("GBC", [128, NCH], F32),
                                 ("B1A", [128, 22], F32), ("B1G", [128, 22], F32),
                                 ("B2C", [128, NCH], F32)):
                t = sub["st"].enter_context(
                    nc.sbuf_tensor(f"{nm}_{l}", list(shp), dt_))
                bu = Buf(None)
                bu.r = list(als)
                names[nm] = (t, bu)
            sub["names"] = names
            return names

        GRPB = [[1, 2], [5, 6], [0, 3], [4, 7]]

        def tp_guard():
            return ("Es", B.cnt["Es"])

        xe_idx = [0]

        def xe_exchange(src_ap, src_bufs):
            j = xe_idx[0]; xe_idx[0] += 1
            ia, oa = f"ccxe{j}ai", f"ccxe{j}ao"
            ib, ob = f"ccxe{j}bi", f"ccxe{j}bo"
            bia, boa, bib, bob = Buf(None), Buf(None), Buf(None), Buf(None)
            V("tensor_scalar", XLS[:, :], src_ap, SNDA[:, :], None, OP.mult,
              reads=tuple(src_bufs) + (bSNDA,), writes=(bXLS,))
            dma("gpsimd", dram[ia].ap(), XLS[:, :], reads=(bXLS,), writes=(bia,),
                q="gp")
            CCall(ia, oa, PAIRS, reads=(bia,), writes=(boa,))
            V("tensor_scalar", XLS[:, :], src_ap, SNDB[:, :], None, OP.mult,
              reads=tuple(src_bufs) + (bSNDB,), writes=(bXLS,))
            dma("gpsimd", dram[ib].ap(), XLS[:, :], reads=(bXLS,), writes=(bib,),
                q="gp")
            CCall(ib, ob, GRPB, reads=(bib,), writes=(bob,))
            dma("gpsimd", XLH[:, :], dram[oa].ap(), reads=(boa,), writes=(bXLH,),
                q="gp")
            V("tensor_scalar", XLH[:, :], XLH[:, :], USEA[:, :], None, OP.mult,
              reads=(bUSEA,), writes=(bXLH,))
            dma("gpsimd", XLS[:, :], dram[ob].ap(), reads=(bob,), writes=(bXLS,),
                q="gp")
            V("scalar_tensor_tensor", XLH[:, :], XLS[:, :], USEB[:, :], XLH[:, :],
              OP.mult, OP.add, reads=(bXLS, bUSEB, bXLH), writes=(bXLH,))
            return bXLH

        def set_shift_cols(halo_buf):
            t = XBF.ap().tensor
            dst = bass.AP(t, 0, [[NCH * (T + 1), 128], [T + 1, NCH], [1, 1]])
            V("tensor_copy", dst, XLH[:, :], reads=(halo_buf,), writes=tuple(bXBF))

        def xsh(ch, lo, hi):
            base = ch * (T + 1) + 1
            off = -1 if ch >= 4 else 0
            return XBF[:, base + lo + off:base + hi + off]

        def ln_shift(GBR, bGBR, GBL, bGBL, GBC, bGBC):
            P4 = psum_f32()
            p4w = pgrp_begin(bP4)
            p2aw = pgrp_begin(bP2a)
            p2bw = pgrp_begin(bP2b)
            for ch in range(NCH):
                V("tensor_tensor", LNS[:, :], xsh(ch, 0, T), xsh(ch, 0, T),
                  OP.mult, reads=(bXBF[ch],), writes=(bLNS,))
                for tt in range(4):
                    mm(P4.ap()[0:1, tt * 512:(tt + 1) * 512], ONE1B[:, 0:1],
                       xsh(ch, tt * 512, (tt + 1) * 512), ch == 0, ch == NCH - 1,
                       reads=(bONE1B, bXBF[ch]), extra=p4w)
                for tt in range(4):
                    dst = (P2a if tt < 2 else P2b).ap()[0:1, (tt % 2) * 512:(tt % 2) * 512 + 512]
                    mm(dst, ONE1[:, 0:1], LNS[:, tt * 512:(tt + 1) * 512],
                       ch == 0, ch == NCH - 1, reads=(bONE1, bLNS),
                       extra=p2aw if tt < 2 else p2bw)
            pgrp_end(bP4); pgrp_end(bP2a); pgrp_end(bP2b)
            V("tensor_scalar", ROWS[0:1, 0:T], P4.ap()[0:1, 0:T], 1.0 / C, None,
              OP.mult, reads=(bP4,), writes=(bROW1,))
            V("tensor_scalar", ROWS[0:1, T:T + 1024], P2a.ap()[0:1, :], 1.0 / C,
              None, OP.mult, reads=(bP2a,), writes=(bROW2,))
            V("tensor_scalar", ROWS[0:1, T + 1024:2 * T], P2b.ap()[0:1, :],
              1.0 / C, None, OP.mult, reads=(bP2b,), writes=(bROW2,))
            V("tensor_tensor", LNS[0:1, 0:T], ROWS[0:1, 0:T], ROWS[0:1, 0:T],
              OP.mult, reads=(bROW1,), writes=(bLNS,))
            V("tensor_tensor", ROWS[0:1, T:2 * T], ROWS[0:1, T:2 * T],
              LNS[0:1, 0:T], OP.subtract, reads=(bROW2, bLNS), writes=(bROW2,))
            S("activation", LNS[0:1, 0:T], ROWS[0:1, T:2 * T], AF.Ln,
              bias=CONSTS[0:1, 0:1], reads=(bROW2, bCONSTS), writes=(bLNS,))
            S("activation", ROWS[0:1, T:2 * T], LNS[0:1, 0:T], AF.Exp, scale=-0.5,
              reads=(bLNS,), writes=(bROW2,))
            V("tensor_tensor", ROWS[0:1, 0:T], ROWS[0:1, 0:T], ROWS[0:1, T:2 * T],
              OP.mult, reads=(bROW2, bROW1), writes=(bROW1,))
            V("memset", GBR[:, :], -1.0, writes=(bGBR,))
            for hf in range(2):
                p2aw = pgrp_begin(bP2a)
                for q in range(2):
                    mm(P2a.ap()[:, q * 512:(q + 1) * 512], ONER[0:1, :],
                       ROWS[0:1, T + hf * 1024 + q * 512:T + hf * 1024 + (q + 1) * 512],
                       True, True, reads=(bONER, bROW2), extra=p2aw)
                pgrp_end(bP2a)
                V("tensor_copy", GBR[0:1, 0:1024],
                  ROWS[0:1, hf * 1024:(hf + 1) * 1024], reads=(bROW1,),
                  writes=(bGBR,))
                for ch in range(NCH):
                    p2bw2 = pgrp_begin(bP2b)
                    for q in range(2):
                        mm(P2b.ap()[:, q * 512:(q + 1) * 512],
                           GBL[:, ch * 128:ch * 128 + 128],
                           GBR[:, q * 512:(q + 1) * 512],
                           True, True, reads=(bGBL, bGBR), extra=p2bw2)
                    pgrp_end(bP2b)
                    sl = slice(hf * 1024, (hf + 1) * 1024)
                    V("tensor_tensor", LNS[:, 0:1024], xsh(ch, hf * 1024, (hf + 1) * 1024),
                      P2a.ap()[:, :], OP.mult, reads=(bXBF[ch], bP2a), writes=(bLNS,))
                    V("scalar_tensor_tensor", XN[:, ch * T + hf * 1024:ch * T + (hf + 1) * 1024],
                      LNS[:, 0:1024], GBC[:, ch:ch + 1], P2b.ap()[:, :],
                      OP.mult, OP.subtract, reads=(bLNS, bGBC, bP2b), writes=(bXN[ch],))


        pmode = {"cur": "f32", "f32": None, "bf": None, "n": 0}
        pregs = {}

        def _pwars():
            d = []
            for b in list(bP4h) + [bP4] + list(pregs.values()):
                d.extend(b.war())
            return d

        def psum_f32():
            return pmode["f32"]

        def psum_tslots():
            w = _pwars()
            t = psum_f32()
            for i in range(4):
                bu = Buf(None)
                bu.r = list(w)
                pregs[i] = bu
            return t

        pmode["f32"] = P4
        P4get = lambda: pmode["f32"]

        def _cut3(tag):
            if CUT == tag or CUT == tag + "nw":
                if CUT == tag:
                    B.wait("gpsimd", [(s, B.cnt[s]) for v in B.pools.values()
                                      for s in v if B.cnt[s]])
                    B.wait("gpsimd", [("Et", B.cnt["Et"]), ("Ev", B.cnt["Ev"]),
                                      ("Es", B.cnt["Es"]), ("CC", B.cnt["CC"])])
                if sub["st"] is not None:
                    stacks_to_close.insert(0, sub["st"])
                    sub["st"] = None
                return True
            return False

        def inner_layer(l, first):
            if _cut3("p30") or _cut3("p30nd") or _cut3("p3"):
                return
            wq = dram[f"l1fw{l}"]
            nm = open_attn(l)
            if _cut3("p3l"):
                return
            (QT, bQT), (KT, bKT), (VT, bVT) = nm["QT"], nm["KT"], nm["VT"]
            (QRA, bQRA), (QRB, bQRB) = nm["QRA"], nm["QRB"]
            (KRA, bKRA), (KRB, bKRB) = nm["KRA"], nm["KRB"]
            (VHAL, bVHAL) = nm["VHAL"]
            (IDN3, bIDN3), (IDF, bIDF) = nm["IDN3"], nm["IDF"]
            (ATT, _bA0) = nm["ATT"]
            ESC, bESC = ATT, None
            bATT = [Buf(None) for _ in range(5)]
            for b in bATT:
                b.r = list(_bA0.r)
            (CSQ, bCSQ), (SSQ, bSSQ) = nm["CSQ"], nm["SSQ"]
            (CSK, bCSK), (SSK, bSSK) = nm["CSK"], nm["SSK"]
            (QSR, bQSR), (KSR, bKSR) = nm["QSR"], nm["KSR"]
            (MSA, bMSA), (MSB, bMSB) = nm["MSA"], nm["MSB"]
            (MLE, bMLE), (MGE, bMGE) = nm["MLE"], nm["MGE"]
            (MFW, bMFW) = nm["MFW"]
            (CH3, bCH3), (CHM, bCHM) = nm["CH3"], nm["CHM"]
            RHA, bRHA = CH3, bCH3
            RHB, bRHB = CHM, bCHM
            if _cut3("p3u"):
                return
            if first:
                for ch in range(NCH):
                    V("tensor_copy", XBF[:, ch * (T + 1) + 1:(ch + 1) * (T + 1)],
                      XN[:, ch * T:(ch + 1) * T], reads=(bXN[ch],),
                      writes=(bXBF[ch],))
                t = XN.ap().tensor
                lastcols = bass.AP(t, T - 1, [[NCH * T, 128], [T, NCH], [1, 1]])
                hb = xe_exchange(lastcols, tuple(bXN))
            else:
                t = XBF.ap().tensor
                lastcols = bass.AP(t, T, [[NCH * (T + 1), 128], [T + 1, NCH], [1, 1]])
                hb = xe_exchange(lastcols, tuple(bXBF))
            set_shift_cols(hb)
            if _cut3("p3a"):
                return

            for g in range(4):
                P4f = psum_f32()
                for tc in range(TCH):
                    if tc % 2 == 0:
                        regs = ((P2a, bP2a, 0), (P2a, bP2a, 512), (P2b, bP2b, 0))
                    else:
                        regs = ((P2b, bP2b, 512), (P4f, bP4h[0], 0),
                                (P4f, bP4h[1], 512))
                    pws = []
                    for (_, rb, _o) in regs:
                        w = list(pgrp_begin(rb))
                        if rb in (bP4h[0], bP4h[1]):
                            w += _pwars()
                        pws.append(tuple(w))
                    for kc in range(NCH):
                        lhs = xsh(kc, tc * 128, (tc + 1) * 128)
                        for e3, m0 in ((0, g * 128), (1, 512 + g * 128),
                                       (2, 1024 + g * 128)):
                            PT, rb, off = regs[e3]
                            w_ap, wb = wtile(wq, kc * 128, 128, m0, 128)
                            mm(PT.ap()[:, off:off + 128], lhs, w_ap,
                               kc == 0, kc == NCH - 1,
                               reads=(wb, bXBF[kc]), extra=pws[e3])
                    for e3 in range(3):
                        pgrp_end(regs[e3][1])
                    for e3, (DT, bDT) in enumerate(((QT, bQT), (KT, bKT),
                                                    (VT, bVT))):
                        PT, rb, off = regs[e3]
                        S("copy", DT[:, tc * 128:(tc + 1) * 128],
                          PT.ap()[:, off:off + 128], reads=(rb,), writes=(bDT,))

                def norm_rot(SRC, bSRC, CS, bCS, SN, bSN, qsr, bqsr):
                    V("tensor_tensor", LNS[:, :], SRC[:, :], SRC[:, :], OP.mult,
                      reads=(bSRC,), writes=(bLNS,))
                    V("tensor_reduce", NRM[:, :, None], LNS.ap().rearrange(
                        "p (a b) -> p a b", b=64), AX.X, OP.add,
                      reads=(bLNS,), writes=(bNRM,))
                    S("activation", NRM[:, :], NRM[:, :], AF.Ln,
                      bias=CONSTS[:, 2:3], reads=(bNRM, bCONSTS), writes=(bNRM,))
                    S("activation", NRM[:, :], NRM[:, :], AF.Exp, scale=-0.5,
                      reads=(bNRM,), writes=(bNRM,))
                    nt = NRM.ap().tensor
                    nv = bass.AP(nt, 0, [[32, 128], [1, 32], [0, 64]])
                    V("tensor_tensor", SRC[:, :], SRC[:, :], nv, OP.mult,
                      reads=(bNRM,), writes=(bSRC,))
                    qv = bass.AP(qsr.ap().tensor, 0, [[128, 128], [0, 16], [1, 128]])
                    V("tensor_tensor", SRC[:, :], SRC[:, :], qv, OP.mult,
                      reads=(bqsr,), writes=(bSRC,))
                    t_ = SRC.ap().tensor
                    s_ = LNS.ap().tensor
                    v1 = bass.AP(t_, 32, [[2048, 128], [64, 32], [1, 32]])
                    d1 = bass.AP(s_, 0, [[2048, 128], [64, 32], [1, 32]])
                    V("tensor_scalar", d1, v1, -1.0, None, OP.mult,
                      reads=(bSRC,), writes=(bLNS,))
                    v2 = bass.AP(t_, 0, [[2048, 128], [64, 32], [1, 32]])
                    d2 = bass.AP(s_, 32, [[2048, 128], [64, 32], [1, 32]])
                    V("tensor_copy", d2, v2, reads=(bSRC,), writes=(bLNS,))
                    cv = bass.AP(CS.ap().tensor, 0,
                                 [[1024, 128], [64, 16], [0, 2], [1, 64]])
                    sv = bass.AP(SN.ap().tensor, 0,
                                 [[1024, 128], [64, 16], [0, 2], [1, 64]])
                    V("tensor_tensor", SRC[:, :], SRC[:, :], cv, OP.mult,
                      reads=(bCS,), writes=(bSRC,))
                    V("tensor_tensor", LNS[:, :], LNS[:, :], sv, OP.mult,
                      reads=(bSN,), writes=(bLNS,))
                    V("tensor_tensor", SRC[:, :], SRC[:, :], LNS[:, :], OP.add,
                      reads=(bLNS,), writes=(bSRC,))

                if _cut3(f"p3qkv{g}"):
                    return
                norm_rot(QT, bQT, CSQ, bCSQ, SSQ, bSSQ, QSR, bQSR)
                norm_rot(KT, bKT, CSK, bCSK, SSK, bSSK, KSR, bKSR)
                if _cut3(f"p3rot{g}"):
                    return

                PTg = psum_tslots()
                esg = tp_guard()
                V("tensor_copy", LNS[:, :], QT[:, :], reads=(bQT,),
                  writes=(bLNS,))
                for tc in range(TCH):
                    for hh, (DQ, bDQ) in enumerate(((QRA, bQRA), (QRB, bQRB))):
                        slot = hh
                        bk = pregs[slot]
                        psl = PTg.ap()[0:64, slot * 512:slot * 512 + 128]
                        gop("tensor",
                            lambda e, s, o=psl,
                            i=LNS[:, tc * 128 + hh * 64:tc * 128 + hh * 64 + 64],
                            d=IDF[:, :]: e.transpose(o, i, d),
                            reads=(bLNS, bIDF), writes=(bk,), extra=(esg,))
                        V("tensor_copy", DQ[0:64, tc * 128:(tc + 1) * 128], psl,
                          reads=(bk,), writes=(bDQ,))
                V("tensor_copy", LNS[:, :], KT[:, :], reads=(bKT,),
                  writes=(bLNS,))
                for tc in range(TCH):
                    for hh, (DK, bDK_) in enumerate(((KRA, bKRA), (KRB, bKRB))):
                        slot = 2 + hh
                        bk = pregs[slot]
                        psl = PTg.ap()[0:64, slot * 512:slot * 512 + 128]
                        gop("tensor",
                            lambda e, s, o=psl,
                            i=LNS[:, tc * 128 + hh * 64:tc * 128 + hh * 64 + 64],
                            d=IDF[:, :]: e.transpose(o, i, d),
                            reads=(bLNS, bIDF), writes=(bk,), extra=(esg,))
                        V("tensor_copy",
                          DK[0:64, 512 + tc * 128:512 + (tc + 1) * 128],
                          psl, reads=(bk,), writes=(bDK_,))

                if _cut3(f"p3tp{g}"):
                    return
                V("tensor_copy", CH3[0:64, 0:512], KRA[0:64, 2048:2560],
                  reads=(bKRA,), writes=(bCH3,))
                V("tensor_copy", CH3[0:64, 512:1024], KRB[0:64, 2048:2560],
                  reads=(bKRB,), writes=(bCH3,))
                V("tensor_copy", CH3[:, 1024:1536], VT[:, 1536:2048],
                  reads=(bVT,), writes=(bCH3,))
                bi3a, bo3a, bi3b, bo3b = Buf(None), Buf(None), Buf(None), Buf(None)
                V("tensor_scalar", CHM[:, :], CH3[:, :], SNDA[:, :], None, OP.mult,
                  reads=(bCH3, bSNDA), writes=(bCHM,))
                dma("gpsimd", dram[f"cc3a{l}{g}i"].ap(), CHM[:, :], reads=(bCHM,),
                    writes=(bi3a,), q="gp")
                CCall(f"cc3a{l}{g}i", f"cc3a{l}{g}o", PAIRS, reads=(bi3a,),
                      writes=(bo3a,))
                V("tensor_scalar", CHM[:, :], CH3[:, :], SNDB[:, :], None, OP.mult,
                  reads=(bCH3, bSNDB), writes=(bCHM,))
                dma("gpsimd", dram[f"cc3b{l}{g}i"].ap(), CHM[:, :], reads=(bCHM,),
                    writes=(bi3b,), q="gp")
                CCall(f"cc3b{l}{g}i", f"cc3b{l}{g}o", GRPB, reads=(bi3b,),
                      writes=(bo3b,))

                P4f = psum_f32()

                def pass1(h, ci):
                    QRx, bQRx = (QRA, bQRA) if h == 0 else (QRB, bQRB)
                    KRx, bKRx = (KRA, bKRA) if h == 0 else (KRB, bKRB)
                    pw = pgrp_begin(bP2a)
                    j0 = ci * 128
                    mm(P2a.ap()[:, 0:512], QRx[0:64, ci * 128:(ci + 1) * 128],
                       KRx[0:64, j0:j0 + 512], True, True,
                       reads=(bQRx, bKRx), extra=pw)
                    mm(P2a.ap()[:, 512:640], QRx[0:64, ci * 128:(ci + 1) * 128],
                       KRx[0:64, j0 + 512:j0 + 640], True, True,
                       reads=(bQRx, bKRx), extra=pw)
                    pgrp_end(bP2a)
                    V("tensor_tensor", P2a.ap()[:, 0:128], P2a.ap()[:, 0:128],
                      MSA[:, :], OP.add, reads=(bMSA,), writes=(bP2a,))
                    V("tensor_tensor", P2a.ap()[:, 512:640], P2a.ap()[:, 512:640],
                      MSB[:, :], OP.add, reads=(bMSB,), writes=(bP2a,))
                    if ci < 4:
                        V("tensor_tensor", P2a.ap()[:, 0:512], P2a.ap()[:, 0:512],
                          MFW[:, ci * 512:(ci + 1) * 512], OP.add,
                          reads=(bMFW,), writes=(bP2a,))
                    V("tensor_reduce", DEN[:, 0:1], P2a.ap()[:, 0:640], AX.X,
                      OP.max, negate=True, reads=(bP2a,), writes=(bDEN,))
                    S("activation", ESC[:, 0:640], P2a.ap()[:, 0:640], AF.Exp,
                      bias=DEN[:, 0:1], accum_out=DEN[:, 1:2],
                      reads=(bP2a, bDEN), writes=(bATT[0], bDEN))
                    S("activation", DEN[:, 2:3], DEN[:, 1:2], AF.Ln,
                      reads=(bDEN,), writes=(bDEN,))
                    V("tensor_tensor", B2S[:, h * 16 + ci:h * 16 + ci + 1],
                      DEN[:, 0:1], DEN[:, 2:3], OP.subtract, reads=(bDEN,),
                      writes=(bB2S,))

                for ci in range(4, TCH):
                    pass1(0, ci)
                    pass1(1, ci)
                if _cut3(f"p3p1{g}"):
                    return
                dma("gpsimd", RHA[:, :], dram[f"cc3a{l}{g}o"].ap(), reads=(bo3a,),
                    writes=(bRHA,), q="gp")
                dma("gpsimd", RHB[:, :], dram[f"cc3b{l}{g}o"].ap(), reads=(bo3b,),
                    writes=(bRHB,), q="gp")
                V("tensor_scalar", RHA[:, :], RHA[:, :], USEA[:, :], None, OP.mult,
                  reads=(bUSEA,), writes=(bRHA,))
                V("scalar_tensor_tensor", RHA[:, :], RHB[:, :], USEB[:, :],
                  RHA[:, :], OP.mult, OP.add, reads=(bRHB, bUSEB, bRHA),
                  writes=(bRHA,))
                V("tensor_copy", KRA[0:64, 0:512], RHA[0:64, 0:512],
                  reads=(bRHA,), writes=(bKRA,))
                V("tensor_copy", KRB[0:64, 0:512], RHA[0:64, 512:1024],
                  reads=(bRHA,), writes=(bKRB,))
                V("tensor_copy", VHAL[:, :], RHA[:, 1024:1536],
                  reads=(bRHA,), writes=(bVHAL,))
                for ci in range(4):
                    pass1(0, ci)
                    pass1(1, ci)
                if _cut3(f"p3hal{g}"):
                    return

                esg = tp_guard()
                for h in range(2):
                    QRx, bQRx = (QRA, bQRA) if h == 0 else (QRB, bQRB)
                    for ci in range(TCH):
                        bk = bP4h[0]
                        psl = P4f.ap()[0:1, (ci % 4) * 512:(ci % 4) * 512 + 128]
                        gop("tensor",
                            lambda e, s, o=psl,
                            i=B2S[:, h * 16 + ci:h * 16 + ci + 1],
                            d=IDF[:, :]: e.transpose(o, i, d),
                            reads=(bB2S, bIDF), writes=(bk,),
                            extra=(esg,) + tuple(_pwars()))
                        V("tensor_copy", QRx[64:65, ci * 128:(ci + 1) * 128], psl,
                          reads=(bk,), writes=(bQRx,))

                if _cut3(f"p3b2{g}"):
                    return
                for h in range(2):
                    QRx, bQRx = (QRA, bQRA) if h == 0 else (QRB, bQRB)
                    KRx, bKRx = (KRA, bKRA) if h == 0 else (KRB, bKRB)
                    prow = slice(0, 64) if h == 0 else slice(64, 128)
                    bOh = bP4h[1]
                    pvw = tuple(pgrp_begin(bOh)) + tuple(_pwars())
                    att_done = {}
                    for jb in range(-4, TCH):
                        i0 = max(jb * 128, 0)
                        i1 = min(jb * 128 + 640, T)
                        par = jb % 2
                        PT = P2b if par == 0 else P2a
                        PB = bP2b if par == 0 else bP2a
                        pw = pgrp_begin(PB)
                        off = 0
                        while i0 + off < i1:
                            nn = min(512, i1 - i0 - off)
                            mm(PT.ap()[:, off:off + nn],
                               KRx[0:65, 512 + jb * 128:512 + jb * 128 + 128],
                               QRx[0:65, i0 + off:i0 + off + nn],
                               True, True,
                               reads=(bKRx, bQRx), extra=pw)
                            off += nn
                        pgrp_end(PB)
                        span = i1 - i0
                        aslot = (jb + 4) % 5
                        bA = bATT[aslot]
                        S("activation", ATT[:, aslot * 640:aslot * 640 + span],
                          PT.ap()[:, 0:span], AF.Exp, reads=(PB,), writes=(bA,))
                        if jb >= 0:
                            V("tensor_tensor",
                              ATT[:, aslot * 640:aslot * 640 + 128],
                              ATT[:, aslot * 640:aslot * 640 + 128], MLE[:, :],
                              OP.mult, reads=(bMLE,), writes=(bA,))
                        ge_ci = jb + 4
                        if ge_ci < TCH:
                            o2 = ge_ci * 128 - i0
                            V("tensor_tensor",
                              ATT[:, aslot * 640 + o2:aslot * 640 + o2 + 128],
                              ATT[:, aslot * 640 + o2:aslot * 640 + o2 + 128],
                              MGE[:, :], OP.mult, reads=(bMGE,), writes=(bA,))
                        att_done[jb] = (aslot, i0)
                        ci = jb
                        if ci < 0:
                            continue
                        for jb2 in range(max(-4, ci - 4), ci + 1):
                            sl2, i02 = att_done[jb2]
                            if jb2 < 0:
                                vl = VHAL[:, (jb2 + 4) * 128 + h * 64:
                                          (jb2 + 4) * 128 + h * 64 + 64]
                                bvl = bVHAL
                            else:
                                vl = VT[:, jb2 * 128 + h * 64:
                                        jb2 * 128 + h * 64 + 64]
                                bvl = bVT
                            mm(P4f.ap()[prow, ci * 128:(ci + 1) * 128], vl,
                               ATT[:, sl2 * 640 + ci * 128 - i02:
                                   sl2 * 640 + (ci + 1) * 128 - i02],
                               jb2 == max(-4, ci - 4), jb2 == ci,
                               reads=(bvl, bATT[sl2]), extra=pvw)
                    pgrp_end(bOh)
                    V("tensor_copy", OFM[prow, g * T:(g + 1) * T],
                      P4f.ap()[prow, :], reads=(bOh,), writes=(bOFM[g],))

            if _cut3(f"p3pv"):
                return
            for och in range(NCH):
                def evw(hf, PT, PB, och=och):
                    bx = bXSTR[och % 2]
                    dma("sync", bx.ap[:, 0:1024],
                        dram["hxi"].ap()[och * 128:(och + 1) * 128,
                                         hf * 1024:(hf + 1) * 1024],
                        reads=(bHXI[och],), writes=(bx,))
                    V("tensor_tensor", WS1[:, 0:1024], PT.ap()[:, :],
                      bx.ap[:, 0:1024], OP.add, reads=(PB, bx), writes=(bWS1,))
                    dma("gpsimd", dram["hxi"].ap()[och * 128:(och + 1) * 128,
                                                   hf * 1024:(hf + 1) * 1024],
                        WS1[:, 0:1024], reads=(bWS1,), writes=(bHXI[och],),
                        q="gp")
                    S("copy", XBF[:, och * (T + 1) + 1 + hf * 1024:
                                  och * (T + 1) + 1 + (hf + 1) * 1024],
                      WS1[:, 0:1024], reads=(bWS1,), writes=(bXBF[och],))
                proj_fm(f"l1fwo{l}", och * 128, evw,
                        rhs_fn=lambda kc, lo, hi: OFM[:, kc * T + lo:kc * T + hi],
                        rbufs=bOFM, nkc=4)

            close_sub()
            nmf = open_ffn(l)
            (HFB, _bh0), (GBUF, bGBUF) = nmf["HFB"], nmf["GBUF"]
            bHFB = [Buf(None) for _ in range(22)]
            for b in bHFB:
                b.r = list(_bh0.r)
            (GBR, bGBR), (GBL, bGBL), (GBC, bGBC) = (
                nmf["GBR"], nmf["GBL"], nmf["GBC"])
            (B1A, bB1A), (B1G, bB1G), (B2C, bB2C) = (
                nmf["B1A"], nmf["B1G"], nmf["B2C"])
            t = XBF.ap().tensor
            lastcols = bass.AP(t, T, [[NCH * (T + 1), 128], [T + 1, NCH], [1, 1]])
            hb = xe_exchange(lastcols, tuple(bXBF))
            set_shift_cols(hb)
            dma("gpsimd", GBL[:, :], dram[f"l1fgb{l}"].ap(), writes=(bGBL,),
                q="gp")
            dma("sync", GBC[:, :], dram[f"l1fgc{l}"].ap(), writes=(bGBC,))
            dma("sync", B1A[:, :], dram[f"l1fb1a{l}"].ap(), writes=(bB1A,))
            dma("sync", B1G[:, :], dram[f"l1fb1g{l}"].ap(), writes=(bB1G,))
            dma("sync", B2C[:, :], dram[f"l1fb2{l}"].ap(), writes=(bB2C,))
            ln_shift(GBR, bGBR, GBL, bGBL, GBC, bGBC)
            if _cut3(f"p3ln"):
                return

            P4f = psum_f32()
            for tf in range(2):
                for mh in range(22):
                    mt = 128 if mh < 21 else 42
                    pga = pgrp_begin(bP2a); pgb = pgrp_begin(bP2b)
                    for kc in range(NCH):
                        wa_ap, wab = wtile(dram[f"l1fw1{l}"], kc * 128, 128,
                                           mh * 128, mt)
                        wg_ap, wgb = wtile(dram[f"l1fw1{l}"], kc * 128, 128,
                                           2730 + mh * 128, mt)
                        for q in range(2):
                            rhs = XN[:, kc * T + tf * 1024 + q * 512:
                                     kc * T + tf * 1024 + (q + 1) * 512]
                            mm(P2a.ap()[0:mt, q * 512:(q + 1) * 512], wa_ap, rhs,
                               kc == 0, kc == NCH - 1, reads=(wab, bXN[kc]),
                               extra=pga)
                            mm(P2b.ap()[0:mt, q * 512:(q + 1) * 512], wg_ap, rhs,
                               kc == 0, kc == NCH - 1, reads=(wgb, bXN[kc]),
                               extra=pgb)
                    pgrp_end(bP2a); pgrp_end(bP2b)
                    S("activation", GBUF[0:mt, :], P2b.ap()[0:mt, :], AF.Gelu,
                      bias=B1G[0:mt, mh:mh + 1], reads=(bP2b, bB1G), writes=(bGBUF,))
                    V("scalar_tensor_tensor", HFB[0:mt, mh * 1024:(mh + 1) * 1024],
                      P2a.ap()[0:mt, :], B1A[0:mt, mh:mh + 1], GBUF[0:mt, :],
                      OP.add, OP.mult, reads=(bP2a, bB1A, bGBUF),
                      writes=(bHFB[mh],))
                for och in range(NCH):
                    par = och % 2
                    pslice = P4f.ap()[:, par * 1024:(par + 1) * 1024]
                    pb4 = bP4h[par]
                    pw = tuple(pgrp_begin(pb4)) + tuple(_pwars())
                    for kc in range(22):
                        kp = 128 if kc < 21 else 42
                        w_ap, wb = wtile(dram[f"l1fw2{l}"], kc * 128, kp,
                                         och * 128, 128)
                        for q in range(2):
                            mm(pslice[:, q * 512:(q + 1) * 512], w_ap,
                               HFB[0:kp, kc * 1024 + q * 512:kc * 1024 + (q + 1) * 512],
                               kc == 0, kc == 21, reads=(wb, bHFB[kc]), extra=pw)
                    pgrp_end(pb4)
                    bx = bXSTR[och % 2]
                    dma("sync", bx.ap[:, 0:1024],
                        dram["hxi"].ap()[och * 128:(och + 1) * 128,
                                         tf * 1024:(tf + 1) * 1024],
                        reads=(bHXI[och],), writes=(bx,))
                    V("scalar_tensor_tensor", WS1[:, 0:1024], pslice,
                      B2C[:, och:och + 1], bx.ap[:, 0:1024], OP.add, OP.add,
                      reads=(pb4, bB2C, bx), writes=(bWS1,))
                    if l == 0:
                        dma("gpsimd", dram["hxi"].ap()[och * 128:(och + 1) * 128,
                                                       tf * 1024:(tf + 1) * 1024],
                            WS1[:, 0:1024], reads=(bWS1,), writes=(bHXI[och],),
                            q="gp")
                        S("copy", XBF[:, och * (T + 1) + 1 + tf * 1024:
                                      och * (T + 1) + 1 + (tf + 1) * 1024],
                          WS1[:, 0:1024], reads=(bWS1,), writes=(bXBF[och],))
                        if dev_stage == "l1f0":
                            dma("gpsimd",
                                dram["y_xi"].ap()[och * 128:(och + 1) * 128,
                                                  tf * 1024:(tf + 1) * 1024],
                                WS1[:, 0:1024], reads=(bWS1,), q="out")
                    else:
                        bx2 = bXSTR[(och + 1) % 2]
                        dma("sync", bx2.ap[:, 1024:2048],
                            dram["hx1"].ap()[och * 128:(och + 1) * 128,
                                             tf * 1024:(tf + 1) * 1024],
                            reads=(bHX1[och],), writes=(bx2,))
                        V("tensor_tensor", WS2[:, 0:1024], WS1[:, 0:1024],
                          bx2.ap[:, 1024:2048], OP.add,
                          reads=(bWS1, bx2), writes=(bWS2,))
                        dma("gpsimd", dram["y"].ap()[och * 128:(och + 1) * 128,
                                                     tf * 1024:(tf + 1) * 1024],
                            WS2[:, 0:1024], reads=(bWS2,), q="out")
            close_sub()

        inner_layer(0, True)
        if dev_stage != "l1f0":
            inner_layer(1, False)

    if CUT == "pad":
        for _ in range(12000):
            V("memset", LNS[0:1, 0:8], 0.0, writes=(bLNS,))
    B.wait("gpsimd", [(s, B.cnt[s]) for v in B.pools.values() for s in v
                      if B.cnt[s]])
    return _finish(nc, st, stP, stacks_to_close, B)


def _finish(nc, st, stP, stacks_to_close, B):
    B.finalize_vals()
    sems = {}
    for name in sorted(B.sem_names):
        sems[name] = st.enter_context(nc.semaphore(name))
    with nc.Block() as block:
        @block.tensor
        def _(e):
            for f in B.prog["tensor"]:
                f(e, sems)

        @block.vector
        def _(e):
            for f in B.prog["vector"]:
                f(e, sems)

        @block.scalar
        def _(e):
            for f in B.prog["scalar"]:
                f(e, sems)

        @block.gpsimd
        def _(e):
            for f in B.prog["gpsimd"]:
                f(e, sems)

        @block.sync
        def _(e):
            for f in B.prog["sync"]:
                f(e, sems)
    for s_ in stacks_to_close:
        s_.close()
    stP.close()
    st.close()
    return nc



def prep_inputs(inputs):
    x = np.asarray(inputs["L0_attn_x"], np.float32)
    xt = np.ascontiguousarray(x.transpose(0, 2, 1))
    vw = np.ascontiguousarray(np.asarray(inputs["L0_attn_value_w"], np.float32).T)
    rw = np.ascontiguousarray(np.asarray(inputs["L0_attn_recep_w"], np.float32).T)
    ow = np.ascontiguousarray(np.asarray(inputs["L0_attn_out_w"], np.float32).T)
    wg = np.ascontiguousarray(np.asarray(inputs["L0_ffn_w_gate"], np.int32).T)
    wu = np.ascontiguousarray(np.asarray(inputs["L0_ffn_w_up"], np.int32).T)
    wd = np.ascontiguousarray(np.asarray(inputs["L0_ffn_w_down"], np.int32).T)
    wdec = np.ascontiguousarray(
        np.asarray(inputs["L0_attn_time_decay"], np.float32).reshape(NCH, 128).T)
    tfir = np.ascontiguousarray(
        np.asarray(inputs["L0_attn_time_first"], np.float32).reshape(NCH, 128).T)
    iota = np.arange(T, dtype=np.float32).reshape(1, T)
    import ml_dtypes
    l1aw = np.ascontiguousarray(np.asarray(inputs["L1_attn_W_qkv"], np.float32))
    l1awf = np.ascontiguousarray(np.asarray(inputs["L1_attn_W_f"], np.float32))
    bf = np.asarray(inputs["L1_attn_b_f"], np.float32)
    l1abf = np.ascontiguousarray(np.tile(bf, 2).reshape(128, 1))
    l1abfr = np.ascontiguousarray(np.tile(bf, 2).reshape(1, 128))
    l1awo = np.ascontiguousarray(np.asarray(inputs["L1_attn_W_out"], np.float32))
    l1abo = np.ascontiguousarray(
        np.asarray(inputs["L1_attn_b_out"], np.float32).reshape(NCH, 128).T)
    idnb = np.eye(128, dtype=ml_dtypes.bfloat16)
    idnf = np.eye(128, dtype=np.float32)
    DH = 64; WINDOW = 512; NEG = np.float32(-1.0e30)
    l1f = {}
    for l in range(2):
        l1f[f"w{l}"] = np.ascontiguousarray(np.asarray(inputs["L1_ffn_wqkv"][l], np.float32))
        l1f[f"wo{l}"] = np.ascontiguousarray(np.asarray(inputs["L1_ffn_wo"][l], np.float32))
        l1f[f"w1{l}"] = np.ascontiguousarray(np.asarray(inputs["L1_ffn_w1"][l], np.float32))
        l1f[f"w2{l}"] = np.ascontiguousarray(np.asarray(inputs["L1_ffn_w2"][l], np.float32))
        l1f[f"gb{l}"] = np.ascontiguousarray(np.stack([
            np.asarray(inputs["L1_ffn_ln_g"][l], np.float32),
            np.asarray(inputs["L1_ffn_ln_b"][l], np.float32)]))
        l1f[f"gc{l}"] = np.ascontiguousarray(
            np.asarray(inputs["L1_ffn_ln_g"][l], np.float32).reshape(NCH, 128).T)
        qs = np.asarray(inputs["L1_ffn_q_scale"][l], np.float32)
        ks = np.asarray(inputs["L1_ffn_k_scale"][l], np.float32)
        l1f[f"qs{l}"] = np.ascontiguousarray(np.tile(qs, 2).reshape(1, 128))
        l1f[f"ks{l}"] = np.ascontiguousarray(np.tile(ks, 2).reshape(1, 128))
        b1 = np.asarray(inputs["L1_ffn_b1"][l], np.float32)
        b1a = np.zeros((22, 128), np.float32); b1g = np.zeros((22, 128), np.float32)
        b1a.reshape(-1)[:2730] = b1[:2730]; b1g.reshape(-1)[:2730] = b1[2730:]
        l1f[f"b1a{l}"] = np.ascontiguousarray(b1a.T)
        l1f[f"b1g{l}"] = np.ascontiguousarray(b1g.T)
        l1f[f"b2{l}"] = np.ascontiguousarray(
            np.asarray(inputs["L1_ffn_b2"][l], np.float32).reshape(NCH, 128).T)
    li = np.arange(128)[:, None]; lj = np.arange(128)[None, :]
    mska = np.where(lj[0][None, :].T * 0 + (np.arange(128)[None, :] >= np.arange(128)[:, None]), 0.0, NEG).astype(ml_dtypes.bfloat16)
    mskb = np.where(np.arange(128)[None, :] <= np.arange(128)[:, None], 0.0, NEG).astype(ml_dtypes.bfloat16)
    mle = (lj >= li).astype(ml_dtypes.bfloat16)
    mge = (lj <= li).astype(ml_dtypes.bfloat16)
    maps = []
    for c in range(NCORES):
        b, h = divmod(c, 2)
        m = {
            "x": np.ascontiguousarray(xt[b, :, h * T:(h + 1) * T]),
            "l0vw": vw, "l0rw": rw, "l0ow": ow,
            "l0g": wg, "l0u": wu, "l0d": wd,
            "l0wd": wdec, "l0tf": tfir,
            "cmask": np.full((128, 1), 1.0 if h == 1 else 0.0, np.float32),
            "iota": iota,
            "l1aw": l1aw, "l1awf": l1awf, "l1abf": l1abf, "l1abfr": l1abfr,
            "l1awo": l1awo, "l1abo": l1abo, "idnb": idnb, "idnf": idnf,
            "mska": mska, "mskb": mskb, "mle": mle, "mge": mge,
        }
        for kk, vv in l1f.items():
            m["l1f" + kk] = vv
        n0 = (b % 2) * 4096 + h * T
        tpos = np.arange(n0, n0 + T, dtype=np.float64)
        half = np.arange(0, DH, 2, dtype=np.float64)
        inv_freq = 1.0 / 10000.0 ** (half / DH)
        freqs = tpos[:, None] * inv_freq[None, :]
        freqs = np.concatenate([freqs, freqs], axis=-1)
        base = (half + 0.4 * DH) / (1.4 * DH)
        power = (tpos - 8192 // 2) / float(WINDOW // 2)
        scale = base[None, :] ** power[:, None]
        scale = np.concatenate([scale, scale], axis=-1)
        cosf = np.cos(freqs); sinf = np.sin(freqs)
        def tbl(a):
            return np.ascontiguousarray(
                a.reshape(16, 128, 64).transpose(1, 0, 2).reshape(128, 16 * 64)
                .astype(ml_dtypes.bfloat16))
        m["csq"] = tbl(8.0 * cosf * scale)
        m["ssq"] = tbl(8.0 * sinf * scale)
        m["csk"] = tbl(cosf / scale)
        m["ssk"] = tbl(sinf / scale)
        fw = np.zeros((4 * 128, 512), np.float32)
        if (b % 2 == 0) and (h == 0):
            for ci in range(4):
                fw[ci * 128:(ci + 1) * 128, 0:512 - 128 * ci] = NEG
        m["mskfw"] = np.ascontiguousarray(
            fw.reshape(4, 128, 512).transpose(1, 0, 2).reshape(128, 2048)
        ).astype(ml_dtypes.bfloat16)
        cg = c % 4
        m["useA"] = np.full((128, 1), 1.0 if cg in (1, 3) else 0.0, np.float32)
        m["useB"] = np.full((128, 1), 1.0 if cg == 2 else 0.0, np.float32)
        m["sendA"] = np.full((128, 1), 1.0 if cg in (0, 2) else 0.0, np.float32)
        m["sendB"] = np.full((128, 1), 1.0 if cg == 1 else 0.0, np.float32)
        maps.append(m)
    return maps


def kernel(**inputs):
    maps = prep_inputs(inputs)
    nc = build_program("full")
    res = run_bass_kernel_spmd(nc, maps, core_ids=list(range(NCORES)))
    out = np.zeros((4, C, 4096), np.float32)
    for c in range(NCORES):
        b, h = divmod(c, 2)
        out[b, :, h * T:(h + 1) * T] = res.results[c]["y"]
    return np.ascontiguousarray(out.transpose(0, 2, 1))
